# version 04
# baseline (speedup 1.0000x reference)
import os
import contextlib
import numpy as np
import ml_dtypes
import concourse.bass as bass
import concourse.mybir as mybir
from concourse.bass_utils import run_bass_kernel_spmd

F32 = mybir.dt.float32
BF16 = mybir.dt.bfloat16
U8 = mybir.dt.uint8
AF = mybir.ActivationFunctionType
ALU = mybir.AluOpType
AX = mybir.AxisListType

B = 2
L = 16384
DM = 2048
DI = 4096
NST = 16
DTR = 128
DCONV = 4
IFF = 8192
NCORE = 8
SPAN = L // 4
HALO = DCONV - 1
CH = 1024
NCH = SPAN // CH
NDT = DI // 128
NDM = DM // 128
NIT = IFF // 128
EPS = 1e-5
TC6 = 512

DBG = set(os.environ.get("KDBG", "").split(",")) - {""}
NPHASE = int(os.environ.get("KPHASES", "9"))
MODE = os.environ.get("KMODE", "full")


class Sem:
    def __init__(self, h):
        self.h = h
        self.n = 0


class Ctx:
    def __init__(self, nc):
        self.nc = nc
        self.q = {"sync": [], "vector": [], "scalar": [], "tensor": [], "gpsimd": []}

    def inc(self, qn, sem, k, fn):
        sem.n += k
        self.q[qn].append(lambda e, fn=fn, s=sem, k=k: fn(e).then_inc(s.h, k))

    def wait(self, qn, sem, n=None):
        v = sem.n if n is None else n
        if v > 0:
            self.q[qn].append(lambda e, s=sem, v=v: e.wait_ge(s.h, v))

    def scope(self, name):
        nc = self.nc
        for qn in self.q:
            self.q[qn].append(lambda e, nc=nc, name=name: nc.push_named_scope(name))

    def escope(self, name):
        nc = self.nc
        for qn in self.q:
            self.q[qn].append(lambda e, nc=nc, name=name: nc.pop_named_scope(name))


class DBuf:
    def __init__(self, ctx, aps):
        self.ctx = ctx
        self.aps = aps
        self.free = [[] for _ in aps]

    def acquire(self, i, qn):
        for (s, v) in self.free[i]:
            self.ctx.wait(qn, s, v)
        self.free[i] = []
        return self.aps[i]

    def release(self, i, sem):
        self.free[i].append((sem, sem.n))


def build_program(mode="full"):
    nc = bass.Bass("TRN2", target_bir_lowering=False, debug=False)

    def dbgkind(name):
        return "ExternalOutput" if name in DBG else "Internal"

    xT = nc.dram_tensor("xT", [DM, SPAN + 4], F32, kind="ExternalInput").ap()
    w_in_x_T = nc.dram_tensor("w_in_x_T", [DM, DI], F32, kind="ExternalInput").ap()
    w_in_z_T = nc.dram_tensor("w_in_z_T", [DM, DI], F32, kind="ExternalInput").ap()
    w_out_T = nc.dram_tensor("w_out_T", [DI, DM], F32, kind="ExternalInput").ap()
    w_x_T = nc.dram_tensor("w_x_T", [DI, 160], F32, kind="ExternalInput").ap()
    w_dt_T = nc.dram_tensor("w_dt_T", [DTR, DI], F32, kind="ExternalInput").ap()
    wg_T = nc.dram_tensor("wg_T", [DM, IFF], U8, kind="ExternalInput").ap()
    wu_T = nc.dram_tensor("wu_T", [DM, IFF], U8, kind="ExternalInput").ap()
    wd_T = nc.dram_tensor("wd_T", [IFF, DM], U8, kind="ExternalInput").ap()
    p_alog = nc.dram_tensor("p_alog", [128, NDT * NST], F32, kind="ExternalInput").ap()
    p_convw = nc.dram_tensor("p_convw", [128, NDT * DCONV], F32, kind="ExternalInput").ap()
    p_convb = nc.dram_tensor("p_convb", [128, NDT], F32, kind="ExternalInput").ap()
    p_bdt = nc.dram_tensor("p_bdt", [128, NDT], F32, kind="ExternalInput").ap()
    p_dskip = nc.dram_tensor("p_dskip", [128, NDT], F32, kind="ExternalInput").ap()
    p_gsc = nc.dram_tensor("p_gsc", [128, NIT], F32, kind="ExternalInput").ap()
    p_usc = nc.dram_tensor("p_usc", [128, NIT], F32, kind="ExternalInput").ap()
    p_dsc = nc.dram_tensor("p_dsc", [128, NDM], F32, kind="ExternalInput").ap()
    p_sel16 = nc.dram_tensor("p_sel16", [16, NST * 128], BF16, kind="ExternalInput").ap()
    p_identb = nc.dram_tensor("p_identb", [128, 128], BF16, kind="ExternalInput").ap()
    p_identf = nc.dram_tensor("p_identf", [128, 128], F32, kind="ExternalInput").ap()
    p_onescol = nc.dram_tensor("p_onescol", [128, 2], F32, kind="ExternalInput").ap()
    p_onescolb = nc.dram_tensor("p_onescolb", [128, 2], BF16, kind="ExternalInput").ap()
    p_row1f = nc.dram_tensor("p_row1f", [1, 128], F32, kind="ExternalInput").ap()
    p_rownegb = nc.dram_tensor("p_rownegb", [1, 128], BF16, kind="ExternalInput").ap()
    p_umask = nc.dram_tensor("p_umask", [128, 2 * (NCORE - 1)], F32,
                             kind="ExternalInput").ap()

    def kmode(name, a_out, b_in):
        if mode == "A":
            return "ExternalOutput" if a_out else dbgkind(name)
        if mode == "B":
            return "ExternalInput" if b_in else dbgkind(name)
        return dbgkind(name)

    d_xconv = nc.dram_tensor("d_xconv", [DI, SPAN], BF16, kind=kmode("d_xconv", 1, 1)).ap()
    d_silz = nc.dram_tensor("d_silz", [DI, SPAN], BF16, kind=kmode("d_silz", 1, 1)).ap()
    d_yloc = nc.dram_tensor("d_yloc", [DI, SPAN], F32, kind=kmode("d_yloc", 1, 1)).ap()
    d_yfin = nc.dram_tensor("d_yfin", [DI, SPAN], BF16, kind=dbgkind("d_yfin")).ap()
    d_resid = nc.dram_tensor("d_resid", [DM, SPAN], F32, kind=dbgkind("d_resid")).ap()
    d_hpr = nc.dram_tensor("d_hpr", [IFF, SPAN], BF16, kind=dbgkind("d_hpr")).ap()
    d_bcm = nc.dram_tensor("d_bcm", [2 * NST, SPAN], F32, kind=kmode("d_bcm", 1, 1)).ap()
    d_dtl = nc.dram_tensor("d_dtl", [DTR, SPAN], F32, kind=kmode("d_dtl", 1, 1)).ap()
    if mode == "B":
        d_ccout = nc.dram_tensor("ccall", [NCORE * DI, 2 * NST], F32,
                                 kind="ExternalInput").ap()
    else:
        d_ccin = nc.dram_tensor("d_ccin", [DI, 2 * NST], F32,
                                kind="ExternalOutput" if mode == "A" else "Internal").ap()
        d_ccout = nc.dram_tensor("d_ccout", [NCORE * DI, 2 * NST], F32,
                                 kind="Internal").ap()
    d_out = nc.dram_tensor("d_out", [SPAN, DM], F32, kind="ExternalOutput").ap()
    if "d_dtlow" in DBG:
        d_dtlow = nc.dram_tensor("d_dtlow", [DTR, SPAN], F32, kind="ExternalOutput").ap()
    if "d_bc" in DBG:
        d_bc = nc.dram_tensor("d_bc", [32, SPAN], F32, kind="ExternalOutput").ap()
    if "d_hdbg" in DBG:
        d_hdbg = nc.dram_tensor("d_hdbg", [128, NST * CH], F32, kind="ExternalOutput").ap()
        d_ddbg = nc.dram_tensor("d_ddbg", [128, 3 * CH + CH], F32, kind="ExternalOutput").ap()
    if "d_hinit" in DBG:
        d_hinit = nc.dram_tensor("d_hinit", [128, NDT * NST], F32,
                                 kind="ExternalOutput").ap()

    with contextlib.ExitStack() as ES:
        E = ES.enter_context

        def sbt(name, shape, dtype):
            return E(nc.sbuf_tensor(name, shape, dtype))

        s_alog = sbt("s_alog", [128, NDT * NST], F32)
        s_convw = sbt("s_convw", [128, NDT * DCONV], F32)
        s_convb = sbt("s_convb", [128, NDT], F32)
        s_bdt = sbt("s_bdt", [128, NDT], F32)
        s_dskip = sbt("s_dskip", [128, NDT], F32)
        s_gsc = sbt("s_gsc", [128, NIT], F32)
        s_usc = sbt("s_usc", [128, NIT], F32)
        s_dsc = sbt("s_dsc", [128, NDM], F32)
        s_sel16 = sbt("s_sel16", [16, NST * 128], BF16)
        s_identb = sbt("s_identb", [128, 128], BF16)
        s_identf = sbt("s_identf", [128, 128], F32)
        s_onescol = sbt("s_onescol", [128, 2], F32)
        s_onescolb = sbt("s_onescolb", [128, 2], BF16)
        s_row1f = sbt("s_row1f", [1, 128], F32)
        s_rownegb = sbt("s_rownegb", [1, 128], BF16)
        s_umask = sbt("s_umask", [128, 2 * (NCORE - 1)], F32)
        s_dtlow = sbt("s_dtlow", [DTR, SPAN], F32)
        s_carry = sbt("s_carry", [128, NDT * NST], F32)
        s_ssum = sbt("s_ssum", [128, NDT], F32)
        s_scarry = sbt("s_scarry", [128, NDT], F32)

        AR_A = sbt("AR_A", [128, 16448], F32)
        AR_BC = sbt("AR_BC", [128, 16384], F32)
        AR_E = sbt("AR_E", [128, 4096], F32)
        AR_S = sbt("AR_S", [128, 8704], F32)

        def carve(arena, w_off, n, dtype, parts=128):
            if dtype == F32:
                return arena[0:parts, w_off:w_off + n]
            if dtype == BF16:
                return arena[0:parts, :].bitcast(BF16)[:, 2 * w_off:2 * w_off + n]
            raise AssertionError

        PA = E(nc.psum_tensor("PA", [128, 1024], F32))
        PB = E(nc.psum_tensor("PB", [128, 1024], F32))
        PC = E(nc.psum_tensor("PC", [128, 512], F32))
        PD = E(nc.psum_tensor("PD", [128, 512], F32))
        PS16 = E(nc.psum_tensor("PS16", [16, 512], F32))
        PS1 = E(nc.psum_tensor("PS1", [1, 512], F32))

        sems = {k: Sem(E(nc.semaphore(k))) for k in
                ["ld", "gld", "st", "te", "act", "dve", "pool", "bar", "cc"]}
        ld, gld, st = sems["ld"], sems["gld"], sems["st"]
        te, act, dve, pool = sems["te"], sems["act"], sems["dve"], sems["pool"]
        bar, ccs = sems["bar"], sems["cc"]

        ctx = Ctx(nc)
        block = E(nc.Block())

        def barrier():
            tgt = bar.n + 5
            for qn in ctx.q:
                ctx.inc(qn, bar, 1, lambda e: e.nop())
            for qn in ctx.q:
                ctx.wait(qn, bar, tgt)

        for tgt, src in [(s_alog, p_alog), (s_convw, p_convw), (s_convb, p_convb),
                         (s_bdt, p_bdt), (s_dskip, p_dskip), (s_gsc, p_gsc),
                         (s_usc, p_usc), (s_dsc, p_dsc), (s_sel16, p_sel16),
                         (s_identb, p_identb), (s_identf, p_identf),
                         (s_onescol, p_onescol), (s_onescolb, p_onescolb),
                         (s_row1f, p_row1f), (s_rownegb, p_rownegb),
                         (s_umask, p_umask)]:
            ctx.inc("sync", ld, 16, lambda e, t=tgt, s=src: e.dma_start(out=t[:], in_=s))
        ctx.wait("scalar", ld)
        ctx.inc("scalar", act, 1,
                lambda e: e.activation(out=s_alog[:], in_=s_alog[:], func=AF.Exp))
        ctx.wait("vector", act)
        ctx.inc("vector", dve, 1,
                lambda e: e.tensor_scalar_mul(s_alog[:], s_alog[:], -1.0))
        for t in ([s_carry, s_ssum, s_scarry] if mode == "B"
                  else [s_dtlow, s_carry, s_ssum, s_scarry]):
            ctx.inc("vector", dve, 1, lambda e, t=t: e.memset(t[:], 0.0))
        barrier()

        s_xh = carve(AR_A, 0, NDM * 2052, BF16)
        wxb_aps = [carve(AR_E, i * 1024, NDM * 128, BF16) for i in range(2)]
        wzb_aps = [carve(AR_E, 2048 + i * 1024, NDM * 128, BF16) for i in range(2)]
        s_xz = carve(AR_S, 0, 1028, BF16)
        s_cacc = carve(AR_S, 514, 1024, F32)
        xcv_aps = [carve(AR_S, 1538 + i * 512, 1024, BF16) for i in range(2)]
        wx_aps = [carve(AR_S, 2562 + i * 80, 160, BF16) for i in range(2)]
        zcv_aps = [carve(AR_S, 2882 + i * 512, 1024, BF16) for i in range(2)]
        s_tails = carve(AR_S, 3906, NDT * 4, BF16)
        s_Bacc = carve(AR_BC, 0, SPAN, F32, parts=16)
        s_Cacc = carve(AR_BC, SPAN, SPAN, F32, parts=16)

        if NPHASE >= 1 and mode != "B":
            ctx.scope("P1")
            ctx.inc("vector", dve, 1, lambda e: e.memset(s_tails, 0.0))
            ctx.inc("vector", dve, 1, lambda e: e.memset(s_Bacc, 0.0))
            ctx.inc("vector", dve, 1, lambda e: e.memset(s_Cacc, 0.0))
            wxblk = DBuf(ctx, wxb_aps)
            wzblk = DBuf(ctx, wzb_aps)
            xcv = DBuf(ctx, xcv_aps)
            zcv = DBuf(ctx, zcv_aps)
            wx = DBuf(ctx, wx_aps)
            for th in range(2):
                c0 = th * 2048
                if th == 1:
                    ctx.wait("gpsimd", te)
                for kt in range(NDM):
                    ctx.inc("gpsimd", gld, 16, lambda e, kt=kt, c0=c0:
                            e.dma_start(out=s_xh[:, kt * 2052: kt * 2052 + 2052],
                                        in_=xT[kt * 128:(kt + 1) * 128, c0:c0 + 2052]))
                for dt in range(NDT):
                    slot = dt % 2
                    wb = wxblk.acquire(slot, "gpsimd")
                    ctx.inc("gpsimd", gld, 16, lambda e, dt=dt, wb=wb:
                            e.dma_start(
                                out=wb.rearrange("p (kt c) -> p kt c", kt=NDM),
                                in_=w_in_x_T.rearrange("(kt p) d -> p kt d", p=128)
                                [:, :, dt * 128:(dt + 1) * 128]))
                    wzb = wzblk.acquire(slot, "gpsimd")
                    ctx.inc("gpsimd", gld, 16, lambda e, dt=dt, wzb=wzb:
                            e.dma_start(
                                out=wzb.rearrange("p (kt c) -> p kt c", kt=NDM),
                                in_=w_in_z_T.rearrange("(kt p) d -> p kt d", p=128)
                                [:, :, dt * 128:(dt + 1) * 128]))
                    wxa = wx.acquire(slot, "gpsimd")
                    ctx.inc("gpsimd", gld, 16, lambda e, dt=dt, wxa=wxa:
                            e.dma_start(out=wxa,
                                        in_=w_x_T[dt * 128:(dt + 1) * 128, :]))
                    for ci in range(2):
                        ch = th * 2 + ci
                        ctx.inc("vector", dve, 1, lambda e, dt=dt:
                                e.tensor_copy(s_xz[:, 0:3],
                                              s_tails[:, dt * 4:dt * 4 + 3]))
                        ctx.wait("tensor", gld)
                        ctx.wait("tensor", act)
                        boff = ci * 1024 + 4
                        for half in range(2):
                            for kt in range(NDM):
                                ctx.inc("tensor", te, 1,
                                        lambda e, kt=kt, wb=wb, boff=boff, half=half:
                                        e.matmul(PA[:, half * 512:(half + 1) * 512],
                                                 wb[:, kt * 128:(kt + 1) * 128],
                                                 s_xh[:, kt * 2052 + boff + half * 512:
                                                      kt * 2052 + boff + (half + 1) * 512],
                                                 start=(kt == 0), stop=(kt == NDM - 1)))
                        ctx.wait("scalar", te)
                        ctx.wait("scalar", dve)
                        ctx.inc("scalar", act, 1, lambda e:
                                e.copy(s_xz[:, 3:1027], PA[:, :]))
                        if th == 0 and ci == 0:
                            ctx.wait("tensor", act)
                            for kt in range(NDM):
                                ctx.inc("tensor", te, 1, lambda e, kt=kt, wb=wb:
                                        e.matmul(PC[:, 0:8],
                                                 wb[:, kt * 128:(kt + 1) * 128],
                                                 s_xh[:, kt * 2052: kt * 2052 + 8],
                                                 start=(kt == 0), stop=(kt == NDM - 1)))
                            ctx.wait("scalar", te)
                            ctx.inc("scalar", act, 1, lambda e:
                                    e.copy(s_xz[:, 0:3], PC[:, 1:4]))
                        ctx.wait("vector", act)
                        ctx.inc("vector", dve, 1, lambda e, dt=dt:
                                e.tensor_copy(s_tails[:, dt * 4:dt * 4 + 3],
                                              s_xz[:, 1024:1027]))
                        ctx.wait("vector", act)
                        ctx.inc("vector", dve, 1, lambda e, dt=dt:
                                e.tensor_scalar(out=s_cacc, in0=s_xz[:, 0:1024],
                                                scalar1=s_convw[:, dt * 4:dt * 4 + 1],
                                                scalar2=None, op0=ALU.mult))
                        for k in range(1, DCONV):
                            ctx.inc("vector", dve, 1, lambda e, dt=dt, k=k:
                                    e.scalar_tensor_tensor(
                                        out=s_cacc, in0=s_xz[:, k:k + 1024],
                                        scalar=s_convw[:, dt * 4 + k:dt * 4 + k + 1],
                                        in1=s_cacc, op0=ALU.mult, op1=ALU.add))
                        xs = ch % 2
                        xa = xcv.acquire(xs, "scalar")
                        ctx.wait("scalar", dve)
                        ctx.inc("scalar", act, 1, lambda e, dt=dt, xa=xa:
                                e.activation(out=xa, in_=s_cacc, func=AF.Silu,
                                             bias=s_convb[:, dt:dt + 1]))
                        ctx.wait("sync", act)
                        ctx.inc("sync", st, 16, lambda e, dt=dt, ch=ch, xa=xa:
                                e.dma_start(out=d_xconv[dt * 128:(dt + 1) * 128,
                                                        ch * CH:(ch + 1) * CH],
                                            in_=xa))
                        xcv.release(xs, st)
                        ctx.wait("tensor", act)
                        ctx.wait("tensor", dve)
                        for half, pt in [(0, PC), (1, PD)]:
                            ctx.inc("tensor", te, 1,
                                    lambda e, wxa=wxa, half=half, pt=pt, xa=xa:
                                    e.matmul(pt[:, :], wxa[:, 0:128],
                                             xa[:, half * 512:(half + 1) * 512],
                                             start=True, stop=True))
                        ctx.wait("vector", te)
                        for half, pt in [(0, PC), (1, PD)]:
                            ctx.inc("vector", dve, 1, lambda e, ch=ch, half=half, pt=pt:
                                    e.tensor_add(
                                        s_dtlow[:, ch * CH + half * 512:
                                                ch * CH + (half + 1) * 512],
                                        s_dtlow[:, ch * CH + half * 512:
                                                ch * CH + (half + 1) * 512],
                                        pt[:, :]))
                        for part, acct in [(0, s_Bacc), (1, s_Cacc)]:
                            for half in range(2):
                                ctx.wait("tensor", dve)
                                ctx.inc("tensor", te, 1,
                                        lambda e, wxa=wxa, part=part, half=half, xa=xa:
                                        e.matmul(PS16[:, :],
                                                 wxa[:, 128 + part * 16:
                                                     128 + (part + 1) * 16],
                                                 xa[:, half * 512:(half + 1) * 512],
                                                 start=True, stop=True))
                                ctx.wait("vector", te)
                                ctx.inc("vector", dve, 1,
                                        lambda e, acct=acct, ch=ch, half=half:
                                        e.tensor_add(
                                            acct[:, ch * CH + half * 512:
                                                 ch * CH + (half + 1) * 512],
                                            acct[:, ch * CH + half * 512:
                                                 ch * CH + (half + 1) * 512],
                                            PS16[:, :]))
                        xcv.release(xs, te)
                        zs = ch % 2
                        za = zcv.acquire(zs, "scalar")
                        ctx.wait("tensor", act)
                        ctx.wait("tensor", dve)
                        for half in range(2):
                            for kt in range(NDM):
                                ctx.inc("tensor", te, 1,
                                        lambda e, kt=kt, wzb=wzb, boff=boff, half=half:
                                        e.matmul(PA[:, half * 512:(half + 1) * 512],
                                                 wzb[:, kt * 128:(kt + 1) * 128],
                                                 s_xh[:, kt * 2052 + boff + half * 512:
                                                      kt * 2052 + boff + (half + 1) * 512],
                                                 start=(kt == 0), stop=(kt == NDM - 1)))
                        ctx.wait("scalar", te)
                        ctx.inc("scalar", act, 1, lambda e, za=za:
                                e.activation(out=za, in_=PA[:, :], func=AF.Silu))
                        ctx.wait("sync", act)
                        ctx.inc("sync", st, 16, lambda e, dt=dt, ch=ch, za=za:
                                e.dma_start(out=d_silz[dt * 128:(dt + 1) * 128,
                                                       ch * CH:(ch + 1) * CH],
                                            in_=za))
                        zcv.release(zs, st)
                    wxblk.release(slot, te)
                    wzblk.release(slot, te)
                    wx.release(slot, te)
            ctx.wait("sync", dve)
            ctx.inc("sync", st, 16, lambda e: e.dma_start(out=d_bcm[0:NST, :], in_=s_Bacc))
            ctx.inc("sync", st, 16, lambda e: e.dma_start(out=d_bcm[NST:2 * NST, :],
                                                          in_=s_Cacc))
            if "d_dtlow" in DBG:
                ctx.wait("sync", dve)
                ctx.inc("sync", st, 16, lambda e: e.dma_start(out=d_dtlow, in_=s_dtlow[:]))
            ctx.wait("sync", st)
            ctx.escope("P1")
            barrier()

        s_brep = carve(AR_BC, 0, NST * CH, BF16)
        s_crep = carve(AR_BC, 8192, NST * CH, BF16)
        s_wdtT = carve(AR_E, 0, DI, F32, parts=DTR)
        s_lnh = carve(AR_BC, 0, NDT * NST, F32)
        s_sgn = carve(AR_BC, 512, NDT * NST, F32)
        s_bt = carve(AR_S, 7680, CH, BF16, parts=NST)
        s_ct = carve(AR_S, 8192, CH, BF16, parts=NST)
        s_delta = carve(AR_S, 0, CH, F32)
        s_sp1 = carve(AR_S, 1024, CH, F32)
        xct_aps = [carve(AR_S, 2048 + i * 512, CH, BF16) for i in range(2)]
        s_vt = carve(AR_S, 3072, CH, BF16)
        s_vb = carve(AR_S, 3584, CH, BF16)
        dA_aps = [carve(AR_S, 4096 + i * 1024, CH, F32) for i in range(2)]
        s_hb = carve(AR_S, 6144, CH, F32)
        prod_aps = [carve(AR_A, 2048 + i * 512, CH, BF16) for i in range(2)]
        yout_aps = [carve(AR_A, i * 1024, CH, F32) for i in range(2)]
        s_S = carve(AR_S, 4096, CH, F32)
        s_onesCH = carve(AR_S, 5120, CH, F32)

        def build_reps(ch, which=("B", "C")):
            ctx.wait("gpsimd", te)
            for wname, srcT in [("B", s_bt), ("C", s_ct)]:
                if wname not in which:
                    continue
                r0 = 0 if wname == "B" else NST
                ctx.inc("gpsimd", gld, 16, lambda e, srcT=srcT, ch=ch, r0=r0:
                        e.dma_start(out=srcT,
                                    in_=d_bcm[r0:r0 + NST, ch * CH:(ch + 1) * CH]))
            ctx.wait("tensor", gld)
            for wname, srcT, dstT in [("B", s_bt, s_brep), ("C", s_ct, s_crep)]:
                if wname not in which:
                    continue
                for n in range(NST):
                    pt = PA if (n % 2 == 0) else PB
                    ctx.wait("tensor", act)
                    ctx.wait("tensor", dve)
                    for half in range(2):
                        ctx.inc("tensor", te, 1,
                                lambda e, n=n, srcT=srcT, pt=pt, half=half:
                                e.matmul(pt[:, half * 512:(half + 1) * 512],
                                         s_sel16[:, n * 128:(n + 1) * 128],
                                         srcT[:, half * 512:(half + 1) * 512],
                                         start=True, stop=True))
                    ctx.wait("scalar", te)
                    ctx.wait("scalar", dve)
                    ctx.inc("scalar", act, 1, lambda e, n=n, dstT=dstT, pt=pt:
                            e.copy(dstT[:, n * CH:(n + 1) * CH], pt[:, :]))

        def make_delta(dt, ch):
            ctx.wait("tensor", act)
            ctx.wait("tensor", dve)
            for half, pt in [(0, PC), (1, PD)]:
                ctx.inc("tensor", te, 1, lambda e, dt=dt, ch=ch, half=half, pt=pt:
                        e.matmul(pt[:, :], s_wdtT[:, dt * 128:(dt + 1) * 128],
                                 s_dtlow[:, ch * CH + half * 512:
                                         ch * CH + (half + 1) * 512],
                                 start=True, stop=True))
            ctx.wait("scalar", te)
            ctx.wait("scalar", dve)
            for half, pt in [(0, PC), (1, PD)]:
                h0, h1 = half * 512, (half + 1) * 512
                ctx.inc("scalar", act, 1, lambda e, pt=pt, h0=h0, h1=h1, dt=dt:
                        e.activation(out=s_sp1[:, h0:h1], in_=pt[:, :], func=AF.Abs,
                                     bias=s_bdt[:, dt:dt + 1]))
                ctx.inc("scalar", act, 1, lambda e, pt=pt, h0=h0, h1=h1, dt=dt:
                        e.activation(out=s_delta[:, h0:h1], in_=pt[:, :], func=AF.Relu,
                                     bias=s_bdt[:, dt:dt + 1]))
            ctx.inc("scalar", act, 1, lambda e:
                    e.activation(out=s_sp1, in_=s_sp1, func=AF.Exp, scale=-1.0))
            ctx.inc("scalar", act, 1, lambda e:
                    e.activation(out=s_sp1, in_=s_sp1, func=AF.Ln, bias=1.0))
            ctx.wait("vector", act)
            ctx.inc("vector", dve, 1, lambda e:
                    e.tensor_add(s_delta, s_delta, s_sp1))

        if NPHASE >= 2 and mode != "B":
            ctx.scope("P2")
            ctx.inc("sync", ld, 16, lambda e: e.dma_start(out=s_wdtT, in_=w_dt_T))
            ctx.wait("tensor", ld)
            xct = DBuf(ctx, xct_aps)
            yout = DBuf(ctx, yout_aps)
            for ch in range(NCH):
                build_reps(ch)
                for dt in range(NDT):
                    slot = dt % 2
                    xa = xct.acquire(slot, "sync")
                    ctx.inc("sync", ld, 16, lambda e, dt=dt, ch=ch, xa=xa:
                            e.dma_start(out=xa,
                                        in_=d_xconv[dt * 128:(dt + 1) * 128,
                                                    ch * CH:(ch + 1) * CH]))
                    make_delta(dt, ch)
                    ctx.wait("vector", ld)
                    ctx.inc("vector", dve, 1, lambda e, xa=xa:
                            e.tensor_mul(s_vt, s_delta, xa))
                    ctx.inc("vector", dve, 1, lambda e:
                            e.tensor_reduce(out=s_sp1[:, 0:1], in_=s_delta,
                                            axis=AX.X, op=ALU.add))
                    ctx.inc("vector", dve, 1, lambda e, dt=dt:
                            e.tensor_add(s_ssum[:, dt:dt + 1], s_ssum[:, dt:dt + 1],
                                         s_sp1[:, 0:1]))
                    for n in range(NST):
                        col = dt * NST + n
                        da = dA_aps[n % 2]
                        pr = prod_aps[n % 2]
                        ctx.wait("scalar", dve, dve.n if n < 2 else dve.n - 5)
                        ctx.inc("scalar", act, 1, lambda e, col=col, da=da:
                                e.activation(out=da, in_=s_delta, func=AF.Exp,
                                             scale=s_alog[:, col:col + 1]))
                        ctx.inc("vector", dve, 1, lambda e, n=n:
                                e.tensor_mul(s_vb, s_vt,
                                             s_brep[:, n * CH:(n + 1) * CH]))
                        ctx.wait("vector", act)
                        ctx.inc("vector", dve, 1, lambda e, col=col, da=da:
                                e.tensor_tensor_scan(out=s_hb, data0=da,
                                                     data1=s_vb,
                                                     initial=s_carry[:, col:col + 1],
                                                     op0=ALU.mult, op1=ALU.add))
                        ctx.inc("vector", dve, 1, lambda e, col=col:
                                e.tensor_copy(s_carry[:, col:col + 1],
                                              s_hb[:, CH - 1:CH]))
                        if "d_hdbg" in DBG and dt == 0 and ch == 0:
                            ctx.wait("sync", dve)
                            ctx.inc("sync", st, 16, lambda e, n=n:
                                    e.dma_start(out=d_hdbg[:, n * CH:(n + 1) * CH],
                                                in_=s_hb[:]))
                            ctx.wait("vector", st)
                            if n == 0:
                                ctx.inc("sync", st, 16, lambda e:
                                        e.dma_start(out=d_ddbg[:, 0:CH], in_=s_delta))
                                ctx.inc("sync", st, 16, lambda e:
                                        e.dma_start(out=d_ddbg[:, CH:2 * CH],
                                                    in_=dA_aps[0]))
                                ctx.inc("sync", st, 16, lambda e:
                                        e.dma_start(out=d_ddbg[:, 2 * CH:3 * CH]
                                                    .bitcast(BF16)[:, 0:CH],
                                                    in_=s_vb))
                                ctx.inc("sync", st, 16, lambda e:
                                        e.dma_start(out=d_ddbg[:, 2 * CH:3 * CH]
                                                    .bitcast(BF16)[:, CH:2 * CH],
                                                    in_=s_brep[:, 0:CH]))
                                ctx.wait("scalar", st)
                                ctx.wait("vector", st)
                        if n >= 2:
                            ctx.wait("vector", te, te.n - 1)
                        ctx.inc("vector", dve, 1, lambda e, n=n, pr=pr:
                                e.tensor_mul(pr, s_hb,
                                             s_crep[:, n * CH:(n + 1) * CH]))
                        ctx.wait("tensor", dve)
                        for half in range(2):
                            ctx.inc("tensor", te, 1, lambda e, n=n, pr=pr, half=half:
                                    e.matmul(PB[:, half * 512:(half + 1) * 512],
                                             s_identb[:],
                                             pr[:, half * 512:(half + 1) * 512],
                                             start=(n == 0), stop=(n == NST - 1)))
                    ys = dt % 2
                    ya = yout.acquire(ys, "vector")
                    ctx.wait("vector", te)
                    ctx.inc("vector", dve, 1, lambda e, dt=dt, xa=xa, ya=ya:
                            e.scalar_tensor_tensor(
                                out=ya, in0=xa, scalar=s_dskip[:, dt:dt + 1],
                                in1=PB[:, :], op0=ALU.mult, op1=ALU.add))
                    xct.release(slot, dve)
                    ctx.wait("sync", dve)
                    ctx.inc("sync", st, 16, lambda e, dt=dt, ch=ch, ya=ya:
                            e.dma_start(out=d_yloc[dt * 128:(dt + 1) * 128,
                                                   ch * CH:(ch + 1) * CH],
                                        in_=ya))
                    yout.release(ys, st)
            ctx.wait("sync", st)
            ctx.escope("P2")
            barrier()

        if NPHASE >= 3 and mode != "B":
            ctx.scope("CCA")
            s_lam = carve(AR_S, 0, NDT * NST, F32)
            for dt in range(NDT):
                ctx.inc("vector", dve, 1, lambda e, dt=dt:
                        e.tensor_scalar(out=s_lam[:, dt * NST:(dt + 1) * NST],
                                        in0=s_alog[:, dt * NST:(dt + 1) * NST],
                                        scalar1=s_ssum[:, dt:dt + 1], scalar2=None,
                                        op0=ALU.mult))
            ctx.wait("scalar", dve)
            ctx.inc("scalar", act, 1, lambda e:
                    e.activation(out=s_lam, in_=s_lam, func=AF.Exp))
            ccr = d_ccin.rearrange("(dt p) c -> p dt c", p=128)
            ctx.wait("sync", act)
            ctx.wait("sync", dve)
            ctx.inc("sync", st, 16, lambda e, ccr=ccr:
                    e.dma_start(out=ccr[:, :, 0:NST],
                                in_=s_carry[:].rearrange("p (dt n) -> p dt n", n=NST)))
            ctx.inc("sync", st, 16, lambda e, ccr=ccr:
                    e.dma_start(out=ccr[:, :, NST:2 * NST],
                                in_=s_lam.rearrange("p (dt n) -> p dt n", n=NST)))
            if mode == "A":
                ctx.wait("sync", dve)
                ctx.inc("sync", st, 16, lambda e: e.dma_start(out=d_dtl, in_=s_dtlow[:]))
            else:
                ctx.wait("gpsimd", st)
                ctx.inc("gpsimd", ccs, 1, lambda e:
                        e.collective_compute(
                            "AllGather", ALU.bypass,
                            replica_groups=[list(range(NCORE))],
                            ins=[d_ccin.opt()], outs=[d_ccout.opt()]))
            ctx.wait("sync", st)
            ctx.escope("CCA")
            barrier()

        if NPHASE >= 3 and mode != "A":
            ctx.scope("CCB")
            s_hinit = carve(AR_S, 512, NDT * NST, F32)
            if mode == "B":
                ctx.inc("sync", ld, 16, lambda e: e.dma_start(out=s_dtlow[:], in_=d_dtl))
            s_cc = carve(AR_BC, 0, NCORE * NDT * 2 * NST, F32)
            if mode == "B":
                ctx.inc("sync", ld, 16, lambda e: e.dma_start(out=s_wdtT, in_=w_dt_T))
            if mode != "B":
                ctx.wait("sync", ccs)
            ctx.inc("sync", ld, 16, lambda e, s_cc=s_cc:
                    e.dma_start(out=s_cc.rearrange("p (r dt c) -> p r dt c",
                                                   r=NCORE, dt=NDT),
                                in_=d_ccout.rearrange("(r dt p) c -> p r dt c",
                                                      p=128, dt=NDT)))
            ctx.wait("vector", ld)
            ctx.inc("vector", dve, 1, lambda e: e.memset(s_hinit, 0.0))
            W2 = 2 * NST
            for r in range(NCORE - 1):
                for dt in range(NDT):
                    base = (r * NDT + dt) * W2
                    ctx.inc("vector", dve, 1, lambda e, base=base, r=r:
                            e.tensor_scalar(
                                out=s_sp1[:, 0:NST],
                                in0=s_cc[:, base + NST: base + 2 * NST],
                                scalar1=s_umask[:, r:r + 1],
                                scalar2=s_umask[:, (NCORE - 1) + r:(NCORE - 1) + r + 1],
                                op0=ALU.mult, op1=ALU.add))
                    ctx.inc("vector", dve, 1, lambda e, dt=dt:
                            e.tensor_mul(s_hinit[:, dt * NST:(dt + 1) * NST],
                                         s_hinit[:, dt * NST:(dt + 1) * NST],
                                         s_sp1[:, 0:NST]))
                    ctx.inc("vector", dve, 1, lambda e, base=base, r=r, dt=dt:
                            e.scalar_tensor_tensor(
                                out=s_hinit[:, dt * NST:(dt + 1) * NST],
                                in0=s_cc[:, base: base + NST],
                                scalar=s_umask[:, r:r + 1],
                                in1=s_hinit[:, dt * NST:(dt + 1) * NST],
                                op0=ALU.mult, op1=ALU.add))
            ctx.wait("scalar", dve)
            ctx.inc("scalar", act, 1, lambda e:
                    e.activation(out=s_lnh, in_=s_hinit, func=AF.Abs))
            ctx.inc("scalar", act, 1, lambda e:
                    e.activation(out=s_lnh, in_=s_lnh, func=AF.Ln))
            ctx.inc("scalar", act, 1, lambda e:
                    e.activation(out=s_sgn, in_=s_hinit, func=AF.Sign))
            ctx.inc("vector", dve, 1, lambda e: e.memset(s_onesCH, 1.0))
            if "d_hinit" in DBG:
                ctx.wait("sync", act)
                ctx.inc("sync", st, 16, lambda e: e.dma_start(out=d_hinit, in_=s_hinit))
                ctx.wait("sync", st)
            ctx.escope("CCB")
            barrier()

        if NPHASE >= 3 and mode != "A":
            ctx.scope("P3")
            hbw_aps = [carve(AR_S, 3072, CH, BF16), carve(AR_S, 3584, CH, BF16)]
            ylt_aps = [carve(AR_A, i * 1024, CH, F32) for i in range(2)]
            zlt_aps = [carve(AR_A, 3072 + i * 512, CH, BF16) for i in range(2)]
            yfb_aps = [carve(AR_A, 4096 + i * 512, CH, BF16) for i in range(2)]
            s_ynz = carve(AR_A, 5120, CH, F32)
            ylt = DBuf(ctx, ylt_aps)
            zlt = DBuf(ctx, zlt_aps)
            yfb = DBuf(ctx, yfb_aps)
            for ch in range(NCH):
                build_reps(ch, which=("C",))
                for dt in range(NDT):
                    slot = dt % 2
                    yla = ylt.acquire(slot, "sync")
                    ctx.inc("sync", ld, 16, lambda e, dt=dt, ch=ch, yla=yla:
                            e.dma_start(out=yla,
                                        in_=d_yloc[dt * 128:(dt + 1) * 128,
                                                   ch * CH:(ch + 1) * CH]))
                    za = zlt.acquire(slot, "sync")
                    ctx.inc("sync", ld, 16, lambda e, dt=dt, ch=ch, za=za:
                            e.dma_start(out=za,
                                        in_=d_silz[dt * 128:(dt + 1) * 128,
                                                   ch * CH:(ch + 1) * CH]))
                    make_delta(dt, ch)
                    ctx.inc("vector", dve, 1, lambda e, dt=dt:
                            e.tensor_tensor_scan(out=s_S, data0=s_onesCH,
                                                 data1=s_delta,
                                                 initial=s_scarry[:, dt:dt + 1],
                                                 op0=ALU.mult, op1=ALU.add))
                    ctx.inc("vector", dve, 1, lambda e, dt=dt:
                            e.tensor_copy(s_scarry[:, dt:dt + 1], s_S[:, CH - 1:CH]))
                    for n in range(NST):
                        col = dt * NST + n
                        pr = prod_aps[n % 2]
                        wslot = hbw_aps[n % 2]
                        ctx.wait("scalar", dve, max(0, dve.n - 1))
                        ctx.inc("scalar", act, 1, lambda e, col=col, wslot=wslot:
                                e.activation(out=wslot, in_=s_S, func=AF.Exp,
                                             scale=s_alog[:, col:col + 1],
                                             bias=s_lnh[:, col:col + 1]))
                        ctx.wait("vector", act)
                        if n >= 2:
                            ctx.wait("vector", te, te.n - 1)
                        ctx.inc("vector", dve, 1, lambda e, col=col, pr=pr, n=n, wslot=wslot:
                                e.scalar_tensor_tensor(
                                    out=pr, in0=wslot,
                                    scalar=s_sgn[:, col:col + 1],
                                    in1=s_crep[:, n * CH:(n + 1) * CH],
                                    op0=ALU.mult, op1=ALU.mult))
                        ctx.wait("tensor", dve)
                        for half in range(2):
                            ctx.inc("tensor", te, 1, lambda e, n=n, pr=pr, half=half:
                                    e.matmul(PB[:, half * 512:(half + 1) * 512],
                                             s_identb[:],
                                             pr[:, half * 512:(half + 1) * 512],
                                             start=(n == 0), stop=(n == NST - 1)))
                    ctx.wait("vector", te)
                    ctx.wait("vector", ld)
                    ctx.inc("vector", dve, 1, lambda e, yla=yla:
                            e.tensor_add(s_ynz, yla, PB[:, :]))
                    ylt.release(slot, dve)
                    ys = dt % 2
                    yfa = yfb.acquire(ys, "vector")
                    ctx.inc("vector", dve, 1, lambda e, yfa=yfa, za=za:
                            e.tensor_mul(yfa, s_ynz, za))
                    zlt.release(slot, dve)
                    ctx.wait("sync", dve)
                    ctx.inc("sync", st, 16, lambda e, dt=dt, ch=ch, yfa=yfa:
                            e.dma_start(out=d_yfin[dt * 128:(dt + 1) * 128,
                                                   ch * CH:(ch + 1) * CH],
                                        in_=yfa))
                    yfb.release(ys, st)
            ctx.wait("sync", st)
            ctx.escope("P3")
            barrier()

        def layer_norm(res_tiles, ntile, width, s_sq, s_row, after_norm):
            ng = width // 512
            dim = ntile * 128
            for g in range(ng):
                ctx.wait("tensor", dve)
                ctx.wait("tensor", act)
                for i in range(ntile):
                    ctx.inc("tensor", te, 1, lambda e, i=i, g=g:
                            e.matmul(PS1[:, :], s_onescol[:, 0:1],
                                     res_tiles(i)[:, g * 512:(g + 1) * 512],
                                     start=(i == 0), stop=(i == ntile - 1)))
                ctx.wait("vector", te)
                ctx.inc("vector", dve, 1, lambda e, g=g:
                        e.tensor_copy(s_row[0:1, g * 512:(g + 1) * 512], PS1[:, :]))
            for g in range(ng):
                for i in range(ntile):
                    ctx.wait("scalar", te)
                    ctx.inc("scalar", act, 1, lambda e, i=i, g=g:
                            e.activation(out=s_sq,
                                         in_=res_tiles(i)[:, g * 512:(g + 1) * 512],
                                         func=AF.Square))
                    ctx.wait("tensor", act)
                    ctx.inc("tensor", te, 1, lambda e, i=i:
                            e.matmul(PS1[:, :], s_onescol[:, 0:1], s_sq,
                                     start=(i == 0), stop=(i == ntile - 1)))
                ctx.wait("vector", te)
                ctx.inc("vector", dve, 1, lambda e, g=g, width=width:
                        e.tensor_copy(s_row[0:1, width + g * 512:width + (g + 1) * 512],
                                      PS1[:, :]))
            ctx.inc("vector", dve, 1, lambda e, width=width, dim=dim:
                    e.tensor_scalar_mul(s_row[0:1, 0:width], s_row[0:1, 0:width],
                                        1.0 / dim))
            ctx.inc("vector", dve, 1, lambda e, width=width, dim=dim:
                    e.tensor_scalar_mul(s_row[0:1, width:2 * width],
                                        s_row[0:1, width:2 * width], 1.0 / dim))
            ctx.inc("vector", dve, 1, lambda e, width=width:
                    e.tensor_mul(s_row[0:1, 2 * width:3 * width],
                                 s_row[0:1, 0:width], s_row[0:1, 0:width]))
            ctx.inc("vector", dve, 1, lambda e, width=width:
                    e.tensor_sub(s_row[0:1, width:2 * width],
                                 s_row[0:1, width:2 * width],
                                 s_row[0:1, 2 * width:3 * width]))
            ctx.inc("vector", dve, 1, lambda e, width=width:
                    e.tensor_scalar_add(s_row[0:1, width:2 * width],
                                        s_row[0:1, width:2 * width], EPS))
            ctx.wait("scalar", dve)
            ctx.inc("scalar", act, 1, lambda e, width=width:
                    e.activation(out=s_row[0:1, width:2 * width],
                                 in_=s_row[0:1, width:2 * width], func=AF.Ln))
            ctx.inc("scalar", act, 1, lambda e, width=width:
                    e.activation(out=s_row[0:1, width:2 * width],
                                 in_=s_row[0:1, width:2 * width], func=AF.Exp,
                                 scale=-0.5))
            ctx.wait("tensor", act)
            ctx.wait("tensor", dve)
            for g in range(ng):
                ctx.inc("tensor", te, 1, lambda e, g=g:
                        e.matmul(PA[:, g * 512:(g + 1) * 512], s_row1f[0:1, :],
                                 s_row[0:1, g * 512:(g + 1) * 512],
                                 start=True, stop=True))
                ctx.inc("tensor", te, 1, lambda e, g=g, width=width:
                        e.matmul(PB[:, g * 512:(g + 1) * 512], s_row1f[0:1, :],
                                 s_row[0:1, width + g * 512:width + (g + 1) * 512],
                                 start=True, stop=True))
            for i in range(ntile):
                ctx.wait("vector", te)
                ctx.inc("vector", dve, 1, lambda e, i=i, width=width:
                        e.tensor_sub(res_tiles(i), res_tiles(i), PA[:, 0:width]))
                ctx.inc("vector", dve, 1, lambda e, i=i, width=width:
                        e.tensor_mul(res_tiles(i), res_tiles(i), PB[:, 0:width]))
                after_norm(i)

        if NPHASE >= 4 and mode != "A":
            ctx.scope("P4")
            s_res4 = carve(AR_A, 0, NDM * CH, F32)
            s_yk = carve(AR_BC, 0, NDT * CH, BF16)
            wob_aps = [carve(AR_E, i * 2048, NDT * 128, BF16) for i in range(2)]
            wob = DBuf(ctx, wob_aps)
            xres_aps = [carve(AR_S, i * 1024, CH, F32) for i in range(2)]
            xres = DBuf(ctx, xres_aps)
            s_sq4 = carve(AR_S, 2048, 512, F32)
            s_row4 = carve(AR_S, 2560, 3 * CH, F32, parts=1)
            for ch in range(NCH):
                ctx.wait("sync", te)
                ctx.wait("vector", st)
                for kt in range(NDT):
                    ctx.inc("sync", ld, 16, lambda e, kt=kt, ch=ch:
                            e.dma_start(out=s_yk[:, kt * CH:(kt + 1) * CH],
                                        in_=d_yfin[kt * 128:(kt + 1) * 128,
                                                   ch * CH:(ch + 1) * CH]))
                for mt in range(NDM):
                    ws = mt % 2
                    wb = wob.acquire(ws, "gpsimd")
                    ctx.inc("gpsimd", gld, 16, lambda e, mt=mt, wb=wb:
                            e.dma_start(
                                out=wb.rearrange("p (kt c) -> p kt c", kt=NDT),
                                in_=w_out_T.rearrange("(kt p) d -> p kt d", p=128)
                                [:, :, mt * 128:(mt + 1) * 128]))
                    xs = mt % 2
                    xa = xres.acquire(xs, "sync")
                    ctx.inc("sync", ld, 16, lambda e, mt=mt, ch=ch, xa=xa:
                            e.dma_start(out=xa,
                                        in_=xT[mt * 128:(mt + 1) * 128,
                                               4 + ch * CH: 4 + (ch + 1) * CH]))
                    ctx.wait("tensor", gld)
                    ctx.wait("tensor", ld)
                    ctx.wait("tensor", dve)
                    for half in range(2):
                        for kt in range(NDT):
                            ctx.inc("tensor", te, 1, lambda e, kt=kt, wb=wb, half=half:
                                    e.matmul(PA[:, half * 512:(half + 1) * 512],
                                             wb[:, kt * 128:(kt + 1) * 128],
                                             s_yk[:, kt * CH + half * 512:
                                                  kt * CH + (half + 1) * 512],
                                             start=(kt == 0), stop=(kt == NDT - 1)))
                    wob.release(ws, te)
                    ctx.wait("vector", te)
                    ctx.wait("vector", ld)
                    ctx.inc("vector", dve, 1, lambda e, mt=mt, xa=xa:
                            e.tensor_add(s_res4[:, mt * CH:(mt + 1) * CH], xa,
                                         PA[:, :]))
                    xres.release(xs, dve)

                def res_tile(i):
                    return s_res4[:, i * CH:(i + 1) * CH]

                def store_norm(i, ch=ch):
                    ctx.wait("sync", dve)
                    ctx.inc("sync", st, 16, lambda e, i=i, ch=ch:
                            e.dma_start(out=d_resid[i * 128:(i + 1) * 128,
                                                    ch * CH:(ch + 1) * CH],
                                        in_=res_tile(i)))

                layer_norm(res_tile, NDM, CH, s_sq4, s_row4, store_norm)
            ctx.wait("sync", st)
            ctx.escope("P4")
            barrier()

        if NPHASE >= 5 and mode != "A":
            ctx.scope("P5")
            TQ = 1024
            s_rq = carve(AR_BC, 0, NDM * TQ, BF16)
            wgu_aps = [carve(AR_E, i * 1024, NDM * 128, BF16) for i in range(4)]
            s_rs = carve(AR_S, 0, TQ, F32, parts=1)
            s_rsb = carve(AR_S, 1024, 2 * TQ, BF16, parts=1)
            s_rslo = carve(AR_S, 2048, TQ, F32, parts=1)
            s_sg5 = carve(AR_S, 3072, TQ, BF16)
            hpo_aps = [carve(AR_S, 3584 + i * 512, TQ, BF16) for i in range(2)]
            hpo = DBuf(ctx, hpo_aps)
            wgu_free = [[], []]
            for tq in range(SPAN // TQ):
                ctx.wait("gpsimd", te)
                for kt in range(NDM):
                    ctx.inc("gpsimd", gld, 16, lambda e, kt=kt, tq=tq:
                            e.dma_start(out=s_rq[:, kt * TQ:(kt + 1) * TQ],
                                        in_=d_resid[kt * 128:(kt + 1) * 128,
                                                    tq * TQ:(tq + 1) * TQ]))
                for g in range(2):
                    ctx.wait("tensor", gld)
                    ctx.wait("tensor", dve)
                    for kt in range(NDM):
                        ctx.inc("tensor", te, 1, lambda e, kt=kt, g=g:
                                e.matmul(PS1[:, :], s_onescolb[:, 0:1],
                                         s_rq[:, kt * TQ + g * 512:
                                              kt * TQ + (g + 1) * 512],
                                         start=(kt == 0), stop=(kt == NDM - 1)))
                    ctx.wait("vector", te)
                    ctx.inc("vector", dve, 1, lambda e, g=g:
                            e.tensor_copy(s_rs[0:1, g * 512:(g + 1) * 512], PS1[:, :]))
                ctx.inc("vector", dve, 1, lambda e:
                        e.tensor_copy(s_rsb[0:1, 0:TQ], s_rs))
                ctx.inc("vector", dve, 1, lambda e:
                        e.tensor_sub(s_rslo, s_rs, s_rsb[0:1, 0:TQ]))
                ctx.inc("vector", dve, 1, lambda e:
                        e.tensor_copy(s_rsb[0:1, TQ:2 * TQ], s_rslo))
                for it in range(NIT):
                    ws = it % 2
                    for (s, v) in wgu_free[ws]:
                        ctx.wait("gpsimd", s, v)
                    wgu_free[ws] = []
                    for wi, wsrc in [(0, wg_T), (1, wu_T)]:
                        sidx = 2 * ws + wi
                        ctx.inc("gpsimd", gld, 16,
                                lambda e, it=it, sidx=sidx, wsrc=wsrc:
                                e.dma_start(
                                    out=wgu_aps[sidx].rearrange("p (kt c) -> p kt c",
                                                                kt=NDM),
                                    in_=wsrc.rearrange("(kt p) d -> p kt d", p=128)
                                    [:, :, it * 128:(it + 1) * 128]))
                    ctx.wait("tensor", gld)
                    ctx.wait("tensor", act)
                    ctx.wait("tensor", dve)
                    for wi, pt in [(0, PA), (1, PB)]:
                        sidx = 2 * ws + wi
                        for half in range(2):
                            for kt in range(NDM):
                                ctx.inc("tensor", te, 1,
                                        lambda e, kt=kt, sidx=sidx, pt=pt, half=half:
                                        e.matmul(pt[:, half * 512:(half + 1) * 512],
                                                 wgu_aps[sidx][:, kt * 128:(kt + 1) * 128],
                                                 s_rq[:, kt * TQ + half * 512:
                                                      kt * TQ + (half + 1) * 512],
                                                 start=(kt == 0), stop=False))
                            for part in range(2):
                                ctx.inc("tensor", te, 1,
                                        lambda e, part=part, pt=pt, half=half:
                                        e.matmul(pt[:, half * 512:(half + 1) * 512],
                                                 s_rownegb[0:1, :],
                                                 s_rsb[0:1, part * TQ + half * 512:
                                                      part * TQ + (half + 1) * 512],
                                                 start=False, stop=(part == 1)))
                    wgu_free[ws].append((te, te.n))
                    ctx.wait("scalar", te)
                    ctx.wait("scalar", dve)
                    ctx.inc("scalar", act, 1, lambda e, it=it:
                            e.activation(out=s_sg5, in_=PA[:, :], func=AF.Silu,
                                         scale=s_gsc[:, it:it + 1]))
                    hs = it % 2
                    ha = hpo.acquire(hs, "vector")
                    ctx.wait("vector", act)
                    ctx.inc("vector", dve, 1, lambda e, it=it, ha=ha:
                            e.scalar_tensor_tensor(
                                out=ha, in0=PB[:, :], scalar=s_usc[:, it:it + 1],
                                in1=s_sg5, op0=ALU.mult, op1=ALU.mult))
                    ctx.wait("sync", dve)
                    ctx.inc("sync", st, 16, lambda e, it=it, tq=tq, ha=ha:
                            e.dma_start(out=d_hpr[it * 128:(it + 1) * 128,
                                                  tq * TQ:(tq + 1) * TQ],
                                        in_=ha))
                    hpo.release(hs, st)
            ctx.wait("sync", st)
            ctx.escope("P5")
            barrier()

        if NPHASE >= 6 and mode != "A":
            ctx.scope("P6")
            s_hch = carve(AR_A, 0, NIT * TC6, BF16)
            s_dnw = carve(AR_BC, 0, NDM * TC6, F32)
            wdb_aps = [carve(AR_E, i * 2048, 8 * TC6, BF16) for i in range(2)]
            s_hs = carve(AR_S, 0, TC6, F32, parts=1)
            s_hsb = carve(AR_S, 512, 2 * TC6, BF16, parts=1)
            s_hslo = carve(AR_S, 1024, TC6, F32, parts=1)
            rrd_aps = [carve(AR_S, 1536 + i * 512, TC6, F32) for i in range(2)]
            rrd = DBuf(ctx, rrd_aps)
            s_sq6 = carve(AR_S, 2560, 512, F32)
            s_row6 = carve(AR_S, 3072, 3 * TC6, F32, parts=1)
            s_osta = carve(AR_S, 4608, DM, F32)
            wdbuf = DBuf(ctx, wdb_aps)
            PQ = [(PA, 0), (PA, 512), (PB, 0), (PB, 512)]
            for ch in range(SPAN // TC6):
                ctx.wait("sync", te)
                for kt in range(NIT):
                    ctx.inc("sync", ld, 16, lambda e, kt=kt, ch=ch:
                            e.dma_start(out=s_hch[:, kt * TC6:(kt + 1) * TC6],
                                        in_=d_hpr[kt * 128:(kt + 1) * 128,
                                                  ch * TC6:(ch + 1) * TC6]))
                ctx.wait("tensor", ld)
                ctx.wait("tensor", dve)
                for kt in range(NIT):
                    ctx.inc("tensor", te, 1, lambda e, kt=kt:
                            e.matmul(PS1[:, :], s_onescolb[:, 0:1],
                                     s_hch[:, kt * TC6:(kt + 1) * TC6],
                                     start=(kt == 0), stop=(kt == NIT - 1)))
                ctx.wait("vector", te)
                ctx.inc("vector", dve, 1, lambda e: e.tensor_copy(s_hs, PS1[:, :]))
                ctx.inc("vector", dve, 1, lambda e:
                        e.tensor_copy(s_hsb[0:1, 0:TC6], s_hs))
                ctx.inc("vector", dve, 1, lambda e:
                        e.tensor_sub(s_hslo, s_hs, s_hsb[0:1, 0:TC6]))
                ctx.inc("vector", dve, 1, lambda e:
                        e.tensor_copy(s_hsb[0:1, TC6:2 * TC6], s_hslo))
                for q in range(4):
                    ctx.wait("tensor", dve)
                    for kg in range(NIT // 8):
                        wslt = kg % 2
                        wb = wdbuf.acquire(wslt, "gpsimd")
                        ctx.inc("gpsimd", gld, 16, lambda e, kg=kg, q=q, wb=wb:
                                e.dma_start(
                                    out=wb.rearrange("p (kt c) -> p kt c", kt=8),
                                    in_=wd_T.rearrange("(kt p) d -> p kt d", p=128)
                                    [:, kg * 8:(kg + 1) * 8,
                                     q * 512:(q + 1) * 512]))
                        ctx.wait("tensor", gld)
                        for kk in range(8):
                            kt = kg * 8 + kk
                            for mq in range(4):
                                pt, po = PQ[mq]
                                ctx.inc("tensor", te, 1,
                                        lambda e, kk=kk, mq=mq, pt=pt, po=po, wb=wb, kt=kt:
                                        e.matmul(pt[:, po:po + 512],
                                                 wb[:, kk * 512 + mq * 128:
                                                    kk * 512 + (mq + 1) * 128],
                                                 s_hch[:, kt * TC6:(kt + 1) * TC6],
                                                 start=(kt == 0), stop=False))
                        wdbuf.release(wslt, te)
                    for mq in range(4):
                        pt, po = PQ[mq]
                        for part in range(2):
                            ctx.inc("tensor", te, 1, lambda e, pt=pt, po=po, part=part:
                                    e.matmul(pt[:, po:po + 512], s_rownegb[0:1, :],
                                             s_hsb[0:1, part * TC6:(part + 1) * TC6],
                                             start=False, stop=(part == 1)))
                    for mq in range(4):
                        mt = q * 4 + mq
                        pt, po = PQ[mq]
                        rsl = mq % 2
                        ra = rrd.acquire(rsl, "sync")
                        ctx.inc("sync", ld, 16, lambda e, mt=mt, ch=ch, ra=ra:
                                e.dma_start(out=ra,
                                            in_=d_resid[mt * 128:(mt + 1) * 128,
                                                        ch * TC6:(ch + 1) * TC6]))
                        ctx.wait("vector", te)
                        ctx.wait("vector", ld)
                        ctx.inc("vector", dve, 1,
                                lambda e, mt=mt, pt=pt, po=po, ra=ra:
                                e.scalar_tensor_tensor(
                                    out=s_dnw[:, mt * TC6:(mt + 1) * TC6],
                                    in0=pt[:, po:po + 512],
                                    scalar=s_dsc[:, mt:mt + 1], in1=ra,
                                    op0=ALU.mult, op1=ALU.add))
                        rrd.release(rsl, dve)

                def res_tile6(i):
                    return s_dnw[:, i * TC6:(i + 1) * TC6]

                def noop(i):
                    pass

                layer_norm(res_tile6, NDM, TC6, s_sq6, s_row6, noop)
                ctx.wait("tensor", dve)
                for tb in range(TC6 // 128):
                    for mt in range(NDM):
                        ctx.wait("tensor", act)
                        ctx.inc("tensor", te, 1, lambda e, mt=mt, tb=tb:
                                e.transpose(PC[:, 0:128],
                                            res_tile6(mt)[:, tb * 128:(tb + 1) * 128],
                                            s_identf[:]))
                        ctx.wait("scalar", te)
                        ctx.wait("scalar", st)
                        ctx.inc("scalar", act, 1, lambda e, mt=mt:
                                e.copy(s_osta[:, mt * 128:(mt + 1) * 128], PC[:, 0:128]))
                    ctx.wait("sync", act)
                    ctx.inc("sync", st, 16, lambda e, ch=ch, tb=tb:
                            e.dma_start(out=d_out[ch * TC6 + tb * 128:
                                                  ch * TC6 + (tb + 1) * 128, :],
                                        in_=s_osta))
            ctx.wait("sync", st)
            ctx.escope("P6")
            barrier()

        ctx.wait("sync", st)

        @block.sync
        def _(e):
            for f in ctx.q["sync"]:
                f(e)

        @block.vector
        def _(e):
            for f in ctx.q["vector"]:
                f(e)

        @block.scalar
        def _(e):
            for f in ctx.q["scalar"]:
                f(e)

        @block.tensor
        def _(e):
            for f in ctx.q["tensor"]:
                f(e)

        @block.gpsimd
        def _(e):
            for f in ctx.q["gpsimd"]:
                f(e)

    return nc


def host_inputs(L0_attn_hidden_states, L0_attn_W_in, L0_attn_conv_w, L0_attn_conv_b,
                L0_attn_W_x, L0_attn_W_dt, L0_attn_b_dt, L0_attn_A_log, L0_attn_D_skip,
                L0_attn_W_out, L0_ffn_gate_w, L0_ffn_gate_scale, L0_ffn_up_w,
                L0_ffn_up_scale, L0_ffn_down_w, L0_ffn_down_scale):
    x = np.asarray(L0_attn_hidden_states)
    asc = np.ascontiguousarray

    def pack128(v):
        return asc(np.asarray(v).reshape(-1, 128).T.astype(np.float32))

    def u8lo(w):
        a = np.ascontiguousarray(np.asarray(w).astype(np.int32))
        return a.view(np.uint8).reshape(a.shape[0], a.shape[1], 4)[:, :, 0]

    w_in = np.asarray(L0_attn_W_in)
    com = dict(
        w_in_x_T=asc(w_in[:DI].T.astype(np.float32)),
        w_in_z_T=asc(w_in[DI:].T.astype(np.float32)),
        w_out_T=asc(np.asarray(L0_attn_W_out).T.astype(np.float32)),
        w_x_T=asc(np.asarray(L0_attn_W_x).T.astype(np.float32)),
        w_dt_T=asc(np.asarray(L0_attn_W_dt).T.astype(np.float32)),
        wg_T=asc(u8lo(L0_ffn_gate_w).T),
        wu_T=asc(u8lo(L0_ffn_up_w).T),
        wd_T=asc(u8lo(L0_ffn_down_w).T),
        p_alog=asc(np.asarray(L0_attn_A_log).astype(np.float32).reshape(NDT, 128, NST)
                   .transpose(1, 0, 2).reshape(128, NDT * NST)),
        p_convw=asc(np.asarray(L0_attn_conv_w).astype(np.float32)
                    .reshape(NDT, 128, DCONV)
                    .transpose(1, 0, 2).reshape(128, NDT * DCONV)),
        p_convb=pack128(L0_attn_conv_b),
        p_bdt=pack128(L0_attn_b_dt),
        p_dskip=pack128(L0_attn_D_skip),
        p_gsc=asc(np.asarray(L0_ffn_gate_scale).astype(np.float32).reshape(NIT, 128).T),
        p_usc=asc(np.asarray(L0_ffn_up_scale).astype(np.float32).reshape(NIT, 128).T),
        p_dsc=asc(np.asarray(L0_ffn_down_scale).astype(np.float32).reshape(NDM, 128).T),
        p_identb=np.eye(128, dtype=ml_dtypes.bfloat16),
        p_identf=np.eye(128, dtype=np.float32),
        p_row1f=np.ones((1, 128), dtype=np.float32),
        p_rownegb=np.full((1, 128), -128.0, dtype=ml_dtypes.bfloat16),
    )
    sel = np.zeros((16, NST * 128), dtype=ml_dtypes.bfloat16)
    for n in range(NST):
        sel[n, n * 128:(n + 1) * 128] = 1.0
    com["p_sel16"] = sel
    com["p_onescol"] = np.ones((128, 2), dtype=np.float32)
    com["p_onescolb"] = np.ones((128, 2), dtype=ml_dtypes.bfloat16)

    in_maps = []
    for c in range(NCORE):
        b, k = c // 4, c % 4
        s0 = k * SPAN
        xs = np.zeros((SPAN + 4, DM), dtype=np.float32)
        lo = max(0, s0 - 4)
        xs[4 - (s0 - lo):] = x[b, lo:s0 + SPAN]
        m = dict(com)
        m["xT"] = asc(xs.T)
        u = np.zeros((128, 2 * (NCORE - 1)), dtype=np.float32)
        for r in range(NCORE - 1):
            is_pred = 1.0 if (r // 4 == b and r < c) else 0.0
            u[:, r] = is_pred
            u[:, (NCORE - 1) + r] = 1.0 - is_pred
        m["p_umask"] = u
        in_maps.append(m)
    return in_maps


def kernel(**inputs):
    in_maps = host_inputs(**inputs)
    if MODE == "split":
        ncA = build_program("A")
        resA = run_bass_kernel_spmd(ncA, in_maps, core_ids=list(range(NCORE)))
        ccall = np.ascontiguousarray(
            np.concatenate([np.asarray(resA.results[c]["d_ccin"]) for c in range(NCORE)]))
        in_maps_b = []
        for c in range(NCORE):
            m = dict(in_maps[c])
            for k in ["d_xconv", "d_silz", "d_yloc", "d_bcm", "d_dtl"]:
                m[k] = np.asarray(resA.results[c][k])
            m["ccall"] = ccall
            in_maps_b.append(m)
        ncB = build_program("B")
        res = run_bass_kernel_spmd(ncB, in_maps_b, core_ids=list(range(NCORE)))
    else:
        nc = build_program("full")
        res = run_bass_kernel_spmd(nc, in_maps, core_ids=list(range(NCORE)))
    out = np.empty((B, L, DM), dtype=np.float32)
    for c in range(NCORE):
        b, k = c // 4, c % 4
        out[b, k * SPAN:(k + 1) * SPAN] = res.results[c]["d_out"]
    return out
